# version 01 (final)
# baseline (speedup 1.0000x reference)
import numpy as np
import ml_dtypes
import concourse.bass as bass
import concourse.mybir as mybir
from concourse.bass_utils import run_bass_kernel_spmd
from contextlib import ExitStack

F32 = mybir.dt.float32
BF16 = mybir.dt.bfloat16
AF = mybir.ActivationFunctionType
ALU = mybir.AluOpType
AX = mybir.AxisListType.X

N_CORES = 8
B, T, D = 2, 8192, 1024
T_OWN = 2048
NT = T_OWN // 128
NB = T // 128
E = 32
H_B, DH_B = 4, 256
BIG = 1.0e30
P3_HEADS = None
PAD_TE = 0
P3_NCH = 16
P3_SKIP = set()
P3_PARTS = {"proj", "conv", "sil", "tok", "nrm"}
P3_TOKMODE = "full"

ENGS = ("tensor", "vector", "scalar", "gpsimd", "sync")


class Sem:
    def __init__(self, nc, st, name):
        self.h = st.enter_context(nc.semaphore(name))
        self.n = 0


class Prog:
    def __init__(self, block):
        self.block = block
        self.q = {e: [] for e in ENGS}

    def _app(self, eng, fn):
        self.q[eng].append(fn)

    def te(self, fn): self._app("tensor", fn)
    def ve(self, fn): self._app("vector", fn)
    def ae(self, fn): self._app("scalar", fn)
    def gp(self, fn): self._app("gpsimd", fn)
    def sy(self, fn): self._app("sync", fn)

    def wait(self, eng, sem, val):
        if val > 0:
            self._app(eng, lambda ee, s=sem.h, v=val: ee.wait_ge(s, v))

    serial_te = None

    def inc(self, eng, fn, sem, k=1):
        if self.serial_te is not None:
            for s_, pname in self.serial_te:
                if pname != eng:
                    self.wait(eng, s_, s_.n)
        sem.n += k

        def wrapped(ee, f=fn, s=sem.h, kk=k):
            f(ee).then_inc(s, kk)
        self._app(eng, wrapped)
        return sem.n

    def flush(self):
        for e in ENGS:
            fns = self.q[e]
            self.q[e] = []
            if fns:
                def runner(ee, fns=fns):
                    for f in fns:
                        f(ee)
                getattr(self.block, e)(runner)


def build_nc(upto=4, tap_phase=None):
    nc = bass.Bass("TRN2", target_bir_lowering=False, num_devices=N_CORES)

    dt = lambda n, s, d_, k: nc.dram_tensor(n, s, d_, kind=k).ap()
    xb = dt("xb", [T, D], F32, "ExternalInput")
    x_own = dt("x_own", [T_OWN, D], F32, "ExternalInput")
    aWq = dt("aWq", [D, D], F32, "ExternalInput")
    aWk = dt("aWk", [D, 256], F32, "ExternalInput")
    aWv = dt("aWv", [D, 256], F32, "ExternalInput")
    aWo = dt("aWo", [D, D], F32, "ExternalInput")
    bWq = dt("bWq", [D, D], F32, "ExternalInput")
    bWk = dt("bWk", [D, D], F32, "ExternalInput")
    bWv = dt("bWv", [D, D], F32, "ExternalInput")
    bWo = dt("bWo", [D, D], F32, "ExternalInput")
    convw = dt("convw", [128, 3, 8, 4], F32, "ExternalInput")
    onorm_bc = dt("onorm_bc", [128, 256], F32, "ExternalInput")
    gwTd = dt("gwTd", [D, E], F32, "ExternalInput")
    esc_bc = dt("esc_bc", [128, E], F32, "ExternalInput")
    guTd = dt("guTd", [E, D, 1024], F32, "ExternalInput")
    dnTd = dt("dnTd", [E, 512, D], F32, "ExternalInput")
    sgTd = dt("sgTd", [D, D], F32, "ExternalInput")
    suTd = dt("suTd", [D, D], F32, "ExternalInput")
    sdTd = dt("sdTd", [D, D], F32, "ExternalInput")
    identc = dt("identc", [128, 128], BF16, "ExternalInput")
    stril_c = dt("stril_c", [128, 128], F32, "ExternalInput")
    striu_c = dt("striu_c", [128, 128], F32, "ExternalInput")
    triu_i_c = dt("triu_i_c", [128, 128], F32, "ExternalInput")
    hsel_c = dt("hsel_c", [128, 8], F32, "ExternalInput")
    ssel_c = dt("ssel_c", [128, 6], F32, "ExternalInput")
    out = dt("out", [T_OWN, D], F32, "ExternalOutput")
    tap = dt("tap", [T_OWN, D], F32, "ExternalOutput") if tap_phase else None
    dWg = (nc.dram_tensor("dWg", [D, 1024], BF16, kind="ExternalOutput").ap()
           if tap_phase in ("p3a", "p3nc") else None)
    tap2 = (nc.dram_tensor("tap2", [T_OWN, D], F32, kind="ExternalOutput"
                           ).ap() if tap_phase in ("p3a", "p3nc") else None)
    if tap_phase == "p1":
        dKt = dt("dKt", [128, 2, T], BF16, "ExternalOutput")
        dV = dt("dV", [128, NB, 256], BF16, "ExternalOutput")
        dQ = dt("dQ", [128, 8, T_OWN], BF16, "ExternalOutput")
        dOT = dt("dOT", [128, 8, 512], BF16, "ExternalOutput")
        dRD = dt("dRD", [1, 2, 512], F32, "ExternalOutput")

    sc = lambda n, s: nc.dram_tensor(n, s, BF16, kind="Internal").ap()
    aWq_b = sc("aWq_b", [D, D]); aWk_b = sc("aWk_b", [D, 256])
    aWv_b = sc("aWv_b", [D, 256]); aWo_b = sc("aWo_b", [D, D])
    bWq_b = sc("bWq_b", [D, D]); bWk_b = sc("bWk_b", [D, D])
    bWv_b = sc("bWv_b", [D, D]); bWo_b = sc("bWo_b", [D, D])
    gwT_b = sc("gwT_b", [D, E])
    guT_b = sc("guT_b", [E, D, 1024]); dnT_b = sc("dnT_b", [E, 512, D])
    sgT_b = sc("sgT_b", [D, D]); suT_b = sc("suT_b", [D, D]); sdT_b = sc("sdT_b", [D, D])
    olocal_d = nc.dram_tensor("olocal_d", [128, NT, D], BF16, kind="Internal").ap()
    qhat_d = nc.dram_tensor("qhat_d", [H_B, 2, 128, T_OWN], BF16, kind="Internal").ap()
    halo_in = nc.dram_tensor("halo_in", [128, 24], BF16, kind="Internal").ap()
    halo_out = nc.dram_tensor("halo_out", [8 * 128, 24], BF16, kind="Internal",
                              addr_space="Shared").ap()
    st_in = nc.dram_tensor("st_in", [128, 4096], BF16, kind="Internal").ap()
    st_out = nc.dram_tensor("st_out", [8 * 128, 4096], BF16, kind="Internal",
                            addr_space="Shared").ap()
    wm_in = nc.dram_tensor("wm_in", [128, 32], BF16, kind="Internal").ap()
    wm_out = nc.dram_tensor("wm_out", [8 * 128, 32], BF16, kind="Internal",
                            addr_space="Shared").ap()

    with ExitStack() as stk:
        ec = stk.enter_context

        def sb(name, shape, dtp=F32):
            return ec(nc.sbuf_tensor(name, shape, dtp)).ap()

        block = ec(nc.Block())
        p = Prog(block)
        sQ = Sem(nc, stk, "sQ")
        sG = Sem(nc, stk, "sG")
        sTE = Sem(nc, stk, "sTE")
        sVE = Sem(nc, stk, "sVE")
        sAE = Sem(nc, stk, "sAE")
        sCC = Sem(nc, stk, "sCC")
        sO = Sem(nc, stk, "sO")
        sVB = Sem(nc, stk, "sVB")
        _vb = {"n": 0}

        def vedge(ve, inst):
            _vb["n"] += 1
            inst.then_inc(sVB.h, 1)
            ve.wait_ge(sVB.h, _vb["n"])
            return inst

        def phase_barrier():
            marks = [(sTE, "tensor"), (sVE, "vector"), (sAE, "scalar")]
            dmas = [(sQ,), (sG,), (sO,)]
            for eng in ENGS:
                for sem, producer in marks:
                    if producer != eng:
                        p.wait(eng, sem, sem.n)
                for (sem,) in dmas:
                    p.wait(eng, sem, sem.n)

        x_res = sb("x_res", [128, NT, 1024], F32)
        ident = sb("ident", [128, 128], BF16)
        stril = sb("stril", [128, 128], F32)
        striu = sb("striu", [128, 128], F32)
        triu_i = sb("triu_i", [128, 128], F32)
        ones_col = sb("ones_col", [128, 1], BF16)
        ones_row = sb("ones_row", [1, 128], BF16)
        onorm_s = sb("onorm_s", [128, 256], F32)
        conv_s = sb("conv_s", [128, 3, 8, 4], F32)
        hsel = sb("hsel", [128, 8], F32)
        ssel = sb("ssel", [128, 6], F32)
        esc_s = sb("esc_s", [128, E], F32)
        mc = sb("mc", [128, 16], F32)

        for d_, s_ in ((ident, identc), (stril, stril_c), (striu, striu_c),
                       (triu_i, triu_i_c), (onorm_s, onorm_bc), (conv_s, convw),
                       (hsel, hsel_c), (ssel, ssel_c), (esc_s, esc_bc)):
            p.inc("sync", lambda sy, d=d_, s=s_: sy.dma_start(out=d, in_=s), sQ, 16)
        const_q = sQ.n

        cast_marks = {}
        cast_list = ([("aWk", aWk_b, aWk), ("aWv", aWv_b, aWv), ("aWq", aWq_b, aWq),
                      ("aWo", aWo_b, aWo), ("gwT", gwT_b, gwTd), ("sgT", sgT_b, sgTd),
                      ("suT", suT_b, suTd), ("sdT", sdT_b, sdTd)]
                     + [w for e_ in range(E)
                        for w in ((f"gu{e_}", guT_b[e_], guTd[e_]),
                                  (f"dn{e_}", dnT_b[e_], dnTd[e_]))]
                     + [("bWq", bWq_b, bWq), ("bWk", bWk_b, bWk),
                        ("bWv", bWv_b, bWv), ("bWo", bWo_b, bWo)])
        m_wm = p.inc("gpsimd", lambda gp: gp.dma_start(
            out=wm_in, in_=identc[:, 0:32]), sG, 16)
        p.wait("gpsimd", sG, m_wm)
        m_wmc = p.inc("gpsimd", lambda gp: gp.collective_compute(
            "AllGather", ALU.bypass,
            replica_groups=[[0, 1, 2, 3, 4, 5, 6, 7]],
            ins=[wm_in.opt()], outs=[wm_out.opt()]), sCC, 1)
        p.wait("gpsimd", sCC, m_wmc)
        for name, d_, s_ in cast_list:
            cast_marks[name] = p.inc(
                "gpsimd", lambda gp, d=d_, s=s_: gp.dma_start(out=d, in_=s), sG, 16)

        def v_init0(ve):
            ve.memset(ones_col, 1.0)
            return ve.memset(ones_row, 1.0)
        p.wait("vector", sQ, const_q)
        ve_const = p.inc("vector", v_init0, sVE)

        def emit_ln(src_ap, dst_bf_ap, idx, eps=1e-5, nd=1024):
            c0 = 4 * (idx % 4)

            def v1(ve):
                vedge(ve, ve.reduce_sum(out=mc[:, c0:c0 + 1], in_=src_ap, axis=AX))
                ve.tensor_scalar(out=mc[:, c0:c0 + 1], in0=mc[:, c0:c0 + 1],
                                 scalar1=1.0 / nd, scalar2=None, op0=ALU.mult)
                vedge(ve, ve.scalar_tensor_tensor(
                    out=dst_bf_ap, in0=src_ap, scalar=1.0,
                    in1=src_ap, op0=ALU.mult, op1=ALU.mult,
                    accum_out=mc[:, c0 + 1:c0 + 2]))
                vedge(ve, ve.tensor_tensor(out=mc[:, c0 + 2:c0 + 3],
                                           in0=mc[:, c0:c0 + 1],
                                           in1=mc[:, c0:c0 + 1], op=ALU.mult))
                vedge(ve, ve.tensor_scalar(out=mc[:, c0 + 1:c0 + 2],
                                           in0=mc[:, c0 + 1:c0 + 2],
                                           scalar1=1.0 / nd,
                                           scalar2=mc[:, c0 + 2:c0 + 3],
                                           op0=ALU.mult, op1=ALU.subtract))
                return ve.tensor_scalar(out=mc[:, c0 + 1:c0 + 2],
                                        in0=mc[:, c0 + 1:c0 + 2],
                                        scalar1=eps, scalar2=None, op0=ALU.add)
            m1 = p.inc("vector", v1, sVE)
            p.wait("scalar", sVE, m1)
            m2 = p.inc("scalar", lambda ae: ae.activation(
                out=mc[:, c0 + 2:c0 + 3], in_=mc[:, c0 + 1:c0 + 2],
                func=AF.Sqrt), sAE)
            p.wait("vector", sAE, m2)

            def v2(ve):
                vedge(ve, ve.reciprocal(out=mc[:, c0 + 1:c0 + 2],
                                        in_=mc[:, c0 + 2:c0 + 3]))
                return ve.tensor_scalar(out=dst_bf_ap, in0=src_ap,
                                        scalar1=mc[:, c0:c0 + 1],
                                        scalar2=mc[:, c0 + 1:c0 + 2],
                                        op0=ALU.subtract, op1=ALU.mult)
            return p.inc("vector", v2, sVE)

        p1outer = ExitStack()
        KtA = p1outer.enter_context(nc.sbuf_tensor("KtA", [128, 2, T], BF16)).ap()
        VA = p1outer.enter_context(nc.sbuf_tensor("VA", [128, NB, 256], BF16)).ap()
        QtA = p1outer.enter_context(nc.sbuf_tensor("QtA", [128, 8, T_OWN], BF16)).ap()

        with ExitStack() as p1s:
            e1 = p1s.enter_context
            sb1 = lambda n, s, d_: e1(nc.sbuf_tensor(n, s, d_)).ap()
            WkS = sb1("WkS", [128, 8, 256], BF16)
            WvS = sb1("WvS", [128, 8, 256], BF16)
            WqS = sb1("WqS", [128, 8, 1024], BF16)
            xraw = sb1("xraw", [128, 2, 1024], F32)
            xlnb = sb1("xlnb", [128, 1024], BF16)
            xlt = sb1("xlt", [128, 2, 8, 128], BF16)
            ptr1 = e1(nc.psum_tensor("ptr1", [128, 1, 1024], BF16)).ap()
            pk = e1(nc.psum_tensor("pk", [128, 2, 512], F32)).ap()
            pv = e1(nc.psum_tensor("pv", [128, 512], F32)).ap()
            pq = e1(nc.psum_tensor("pq", [128, 2, 512], F32)).ap()

            p.wait("sync", sG, cast_marks["aWq"])
            p.inc("sync", lambda sy: sy.dma_start(
                out=WkS, in_=aWk_b.rearrange("(c p) f -> p c f", p=128)), sQ, 16)
            p.inc("sync", lambda sy: sy.dma_start(
                out=WvS, in_=aWv_b.rearrange("(c p) f -> p c f", p=128)), sQ, 16)
            w_kv = p.inc("sync", lambda sy: sy.dma_start(
                out=WqS, in_=aWq_b.rearrange("(c p) f -> p c f", p=128)), sQ, 16)

            NCK = NB
            NTOT = NCK + NT
            ln_m = [0] * NTOT
            tr_m = [0] * NTOT
            xlt_m = [0] * NTOT
            prj_m = [0] * NTOT
            ev_m = [0] * NTOT

            for ck in range(NCK):
                bufc = ck % 2
                if ck >= 2:
                    p.wait("sync", sVE, ln_m[ck - 2])
                m_dma = p.inc("sync", lambda sy, ck=ck, bufc=bufc: sy.dma_start(
                    out=xraw[:, bufc],
                    in_=xb.rearrange("(n p) f -> n p f", p=128)[ck]), sQ, 16)
                p.wait("vector", sQ, m_dma)
                if ck >= 1:
                    p.wait("vector", sTE, tr_m[ck - 1])
                emit_ln(xraw[:, bufc], xlnb, ck)
                ln_m[ck] = sVE.n
                p.wait("tensor", sVE, ln_m[ck])
                p.wait("tensor", sVE, ve_const)
                if ck >= 1:
                    p.wait("tensor", sVE, xlt_m[ck - 1])

                def t_tr(te):
                    r = None
                    for hc in range(8):
                        r = te.transpose(out=ptr1[:, 0, 128 * hc:128 * (hc + 1)],
                                         in_=xlnb[:, 128 * hc:128 * (hc + 1)],
                                         identity=ident)
                    return r
                tr_m[ck] = p.inc("tensor", t_tr, sTE)
                p.wait("vector", sTE, tr_m[ck])
                if ck >= 2:
                    p.wait("vector", sTE, prj_m[ck - 2])

                def v_tr(ve, bufc=bufc):
                    return ve.tensor_copy(
                        out=xlt[:, bufc],
                        in_=ptr1[:, 0].rearrange("p (c f) -> p c f", c=8))
                xlt_m[ck] = p.inc("vector", v_tr, sVE)

                p.wait("tensor", sVE, xlt_m[ck])
                p.wait("tensor", sQ, w_kv)
                if ck >= 1:
                    p.wait("tensor", sVE, ev_m[ck - 1])

                def t_kv(te, bufc=bufc):
                    for half in range(2):
                        for hc in range(8):
                            te.matmul(out=pk[:, half, 0:128],
                                      lhsT=WkS[:, hc, 128 * half:128 * (half + 1)],
                                      rhs=xlt[:, bufc, hc],
                                      start=(hc == 0), stop=(hc == 7))
                    r = None
                    for hc in range(8):
                        r = te.matmul(out=pv[:, 0:256],
                                      lhsT=xlt[:, bufc, hc],
                                      rhs=WvS[:, hc],
                                      start=(hc == 0), stop=(hc == 7))
                    return r
                prj_m[ck] = p.inc("tensor", t_kv, sTE)
                p.wait("vector", sTE, prj_m[ck])

                def v_kv(ve, ck=ck):
                    for half in range(2):
                        ve.tensor_copy(out=KtA[:, half, 128 * ck:128 * (ck + 1)],
                                       in_=pk[:, half, 0:128])
                    return ve.tensor_copy(out=VA[:, ck], in_=pv[:, 0:256])
                ev_m[ck] = p.inc("vector", v_kv, sVE)

            for tt in range(NT):
                ii = NCK + tt
                bufc = tt % 2
                p.wait("sync", sVE, ln_m[ii - 2])
                p.wait("sync", sTE, tr_m[ii - 1])
                m_dma = p.inc("sync", lambda sy, tt=tt, bufc=bufc: sy.dma_start(
                    out=xraw[:, bufc],
                    in_=x_own.rearrange("(n p) f -> n p f", p=128)[tt]), sQ, 16)
                p.wait("vector", sQ, m_dma)
                p.wait("vector", sTE, tr_m[ii - 1])
                p.inc("vector", lambda ve, tt=tt, bufc=bufc: ve.tensor_copy(
                    out=x_res[:, tt], in_=xraw[:, bufc]), sVE)
                emit_ln(xraw[:, bufc], xlnb, tt)
                ln_m[ii] = sVE.n
                p.wait("tensor", sVE, ln_m[ii])
                p.wait("tensor", sVE, xlt_m[ii - 1])

                def t_tr2(te):
                    r = None
                    for hc in range(8):
                        r = te.transpose(out=ptr1[:, 0, 128 * hc:128 * (hc + 1)],
                                         in_=xlnb[:, 128 * hc:128 * (hc + 1)],
                                         identity=ident)
                    return r
                tr_m[ii] = p.inc("tensor", t_tr2, sTE)
                p.wait("vector", sTE, tr_m[ii])
                if tt >= 2:
                    p.wait("vector", sTE, prj_m[ii - 2])
                xlt_m[ii] = p.inc("vector", lambda ve, bufc=bufc: ve.tensor_copy(
                    out=xlt[:, bufc],
                    in_=ptr1[:, 0].rearrange("p (c f) -> p c f", c=8)), sVE)

                p.wait("tensor", sVE, xlt_m[ii])
                last_ev = ev_m[ii - 1]
                for qph in range(4):
                    p.wait("tensor", sVE, last_ev)

                    def t_q(te, bufc=bufc, qph=qph):
                        r = None
                        for half in range(2):
                            qc = 2 * qph + half
                            for hc in range(8):
                                r = te.matmul(
                                    out=pq[:, half, 0:128],
                                    lhsT=WqS[:, hc, 128 * qc:128 * (qc + 1)],
                                    rhs=xlt[:, bufc, hc],
                                    start=(hc == 0), stop=(hc == 7))
                        return r
                    m_q = p.inc("tensor", t_q, sTE)
                    p.wait("vector", sTE, m_q)

                    def v_q(ve, tt=tt, qph=qph):
                        r = None
                        for half in range(2):
                            qc = 2 * qph + half
                            for sub in range(2):
                                h = 2 * qc + sub
                                db = 64 * ((h // 4) % 2)
                                qtile = (h % 4) + 4 * (h // 8)
                                r = ve.tensor_copy(
                                    out=QtA[db:db + 64, qtile,
                                            128 * tt:128 * (tt + 1)],
                                    in_=pq[64 * sub:64 * (sub + 1), half, 0:128])
                        return r
                    last_ev = p.inc("vector", v_q, sVE)
                prj_m[ii] = sTE.n
                ev_m[ii] = last_ev
            p.flush()

        phase_barrier()
        with ExitStack() as p1b:
            e2 = p1b.enter_context
            sb2 = lambda n, s, d_: e2(nc.sbuf_tensor(n, s, d_)).ap()
            WoS = sb2("WoS", [128, 8, 1024], BF16)
            OTs = sb2("OTs", [128, 8, 512], BF16)
            expS = sb2("expS", [128, 2, 1024], BF16)
            rdf = sb2("rdf", [1, 2, 512], F32)
            rdb = sb2("rdb", [1, 2, 512], BF16)
            rbS = sb2("rbS", [128, 512], BF16)
            pSC = e2(nc.psum_tensor("pSC", [128, 2, 1024], F32)).ap()
            pAV = e2(nc.psum_tensor("pAV", [128, 2, 512], F32)).ap()
            pDR = e2(nc.psum_tensor("pDR", [128, 2, 512], F32)).ap()

            p.wait("sync", sG, cast_marks["aWo"])
            m_wo = p.inc("sync", lambda sy: sy.dma_start(
                out=WoS, in_=aWo_b.rearrange("(c p) f -> p c f", p=128)), sQ, 16)

            NKK = 64
            exp_seq = []
            av_seq = []
            wo_mm_last = 0
            wo_add_last = 0
            epi_last = 0
            for qq in range(4):
                for pr in range(8):
                    hA, hB = 2 * pr, 2 * pr + 1
                    gA, gB = hA // 4, hB // 4
                    av_marks = [0] * NKK

                    for kt in range(NKK):
                        bsc = kt % 2
                        gidx = len(exp_seq)
                        if gidx >= 2:
                            p.wait("tensor", sAE, exp_seq[gidx - 2])
                        if kt < 2 and pr == 0 and qq > 0:
                            p.wait("tensor", sVE, wo_add_last)
                        if kt < 2:
                            p.wait("tensor", sVE, epi_last)

                        def t_sc(te, kt=kt, bsc=bsc, hA=hA, hB=hB, gA=gA, gB=gB, qq=qq):
                            bA, bB = 64 * (gA % 2), 64 * (gB % 2)
                            qtA = (hA % 4) + 4 * (hA // 8)
                            qtB = (hB % 4) + 4 * (hB // 8)
                            te.matmul(
                                out=pSC[:, bsc, 0:512],
                                lhsT=KtA[bA:bA + 64, gA // 2, 128 * kt:128 * (kt + 1)],
                                rhs=QtA[bA:bA + 64, qtA, 512 * qq:512 * (qq + 1)],
                                start=True, stop=True)
                            return te.matmul(
                                out=pSC[:, bsc, 512:1024],
                                lhsT=KtA[bB:bB + 64, gB // 2, 128 * kt:128 * (kt + 1)],
                                rhs=QtA[bB:bB + 64, qtB, 512 * qq:512 * (qq + 1)],
                                start=True, stop=True)
                        m_sc = p.inc("tensor", t_sc, sTE)

                        p.wait("scalar", sTE, m_sc)
                        if len(av_seq) >= 2:
                            p.wait("scalar", sTE, av_seq[-2])
                        m_exp = p.inc("scalar", lambda ae, bsc=bsc: ae.activation(
                            out=expS[:, bsc], in_=pSC[:, bsc], func=AF.Exp,
                            scale=0.125), sAE)
                        exp_seq.append(m_exp)

                        p.wait("tensor", sAE, m_exp)
                        if kt == 0:
                            p.wait("tensor", sVE, epi_last)

                        def t_av(te, kt=kt, bsc=bsc, gA=gA, gB=gB):
                            te.matmul(out=pAV[0:64, 0, :],
                                      lhsT=VA[:, kt, 64 * gA:64 * gA + 64],
                                      rhs=expS[:, bsc, 0:512],
                                      start=(kt == 0), stop=(kt == NKK - 1),
                                      tile_position=(0, 0))
                            te.matmul(out=pAV[64:128, 1, :],
                                      lhsT=VA[:, kt, 64 * gB:64 * gB + 64],
                                      rhs=expS[:, bsc, 512:1024],
                                      start=(kt == 0), stop=(kt == NKK - 1),
                                      tile_position=(0, 64))
                            te.matmul(out=pDR[0:1, 0, :], lhsT=ones_col,
                                      rhs=expS[:, bsc, 0:512],
                                      start=(kt == 0), stop=(kt == NKK - 1),
                                      tile_position=(0, 0))
                            return te.matmul(out=pDR[32:33, 1, :], lhsT=ones_col,
                                             rhs=expS[:, bsc, 512:1024],
                                             start=(kt == 0), stop=(kt == NKK - 1),
                                             tile_position=(0, 32))
                        av_marks[kt] = p.inc("tensor", t_av, sTE)
                        av_seq.append(av_marks[kt])

                    p.wait("vector", sTE, av_marks[NKK - 1])
                    if pr == 0:
                        p.wait("vector", sTE, wo_mm_last)

                    def v_rd(ve):
                        ve.reciprocal(out=rdf[:, 0], in_=pDR[0:1, 0, :])
                        vedge(ve, ve.reciprocal(out=rdf[:, 1], in_=pDR[32:33, 1, :]))
                        return ve.tensor_copy(out=rdb, in_=rdf)
                    m_rd = p.inc("vector", v_rd, sVE)
                    p.wait("tensor", sVE, m_rd)

                    def t_rb(te):
                        te.matmul(out=pSC[0:64, 0, 0:512], lhsT=ones_row[:, 0:64],
                                  rhs=rdb[:, 0], start=True, stop=True,
                                  tile_position=(0, 0))
                        return te.matmul(out=pSC[64:128, 0, 512:1024],
                                         lhsT=ones_row[:, 0:64],
                                         rhs=rdb[:, 1], start=True, stop=True,
                                         tile_position=(0, 64))
                    m_rb = p.inc("tensor", t_rb, sTE)
                    p.wait("vector", sTE, m_rb)

                    def v_epi(ve, hA=hA, hB=hB):
                        ve.tensor_copy(out=rbS[0:64, :], in_=pSC[0:64, 0, 0:512])
                        vedge(ve, ve.tensor_copy(out=rbS[64:128, :],
                                                 in_=pSC[64:128, 0, 512:1024]))
                        r = None
                        for h, r0 in ((hA, 0), (hB, 64)):
                            base = 64 * (h % 2)
                            r = ve.tensor_tensor(
                                out=OTs[base:base + 64, h // 2, :],
                                in0=pAV[r0:r0 + 64, r0 // 64, :],
                                in1=rbS[r0:r0 + 64, :], op=ALU.mult)
                        return r
                    epi_last = p.inc("vector", v_epi, sVE)

                if tap_phase == "p1" and qq == 0:
                    p.wait("sync", sVE, epi_last)
                    p.inc("sync", lambda sy: sy.dma_start(out=dOT, in_=OTs), sO, 16)
                    p.inc("sync", lambda sy: sy.dma_start(out=dRD, in_=rdf), sO, 16)
                p.wait("tensor", sVE, epi_last)
                p.wait("tensor", sQ, m_wo)
                wo_adds = [0] * 4
                for tsub in range(4):
                    tile = 4 * qq + tsub
                    if tsub >= 2:
                        p.wait("tensor", sVE, wo_adds[tsub - 2])

                    def t_wo(te, tsub=tsub):
                        r = None
                        for oh in range(2):
                            for oc in range(8):
                                r = te.matmul(
                                    out=pSC[:, tsub % 2, 512 * oh:512 * (oh + 1)],
                                    lhsT=OTs[:, oc, 128 * tsub:128 * (tsub + 1)],
                                    rhs=WoS[:, oc, 512 * oh:512 * (oh + 1)],
                                    start=(oc == 0), stop=(oc == 7))
                        return r
                    m_womm = p.inc("tensor", t_wo, sTE)
                    p.wait("vector", sTE, m_womm)
                    wo_adds[tsub] = p.inc(
                        "vector", lambda ve, tile=tile, tsub=tsub: ve.tensor_tensor(
                            out=x_res[:, tile], in0=x_res[:, tile],
                            in1=pSC[:, tsub % 2], op=ALU.add), sVE)
                wo_mm_last = sTE.n
                wo_add_last = wo_adds[3]
            if tap_phase == "p1":
                p.wait("sync", sVE, sVE.n)
                mdbg = p.inc("sync", lambda sy: sy.dma_start(out=dKt, in_=KtA), sO, 16)
                mdbg = p.inc("sync", lambda sy: sy.dma_start(out=dV, in_=VA), sO, 16)
                mdbg = p.inc("sync", lambda sy: sy.dma_start(out=dQ, in_=QtA), sO, 16)
                p.wait("sync", sO, mdbg)
            phase_barrier()
            p.flush()
        p1outer.close()

        if tap_phase == "p1":
            emit_store(p, x_res, tap, sO, sVE)
            p.flush()
            return nc

        def emit_ffn(tag):
            with ExitStack() as pf:
                ef = pf.enter_context
                sbf = lambda n, s, d_: ef(nc.sbuf_tensor(n, s, d_)).ap()
                xlnT = sbf(f"xlnT{tag}", [128, 8, 2048], BF16)
                gwS = sbf(f"gwS{tag}", [128, 8, E], BF16)
                cst = sbf(f"cst{tag}", [128, NT, E], F32)
                rt = sbf(f"rt{tag}", [128, 6 * E], F32)
                xlnb2 = sbf(f"xlnb2{tag}", [128, 1024], BF16)

                p.wait("sync", sG, cast_marks["gwT"])
                m_gw = p.inc("sync", lambda sy: sy.dma_start(
                    out=gwS, in_=gwT_b.rearrange("(c p) f -> p c f", p=128)), sQ, 16)

                with ExitStack() as pr_:
                    er = pr_.enter_context
                    ptrF = er(nc.psum_tensor(f"ptrF{tag}", [128, 1024], BF16)).ap()
                    pL = er(nc.psum_tensor(f"pL{tag}", [128, E], F32)).ap()
                    m_xv = 0
                    m_sg = 0
                    m_rtv = 0
                    for tt in range(NT):
                        m_ln = emit_ln(x_res[:, tt], xlnb2, tt)
                        p.wait("tensor", sVE, m_ln)
                        p.wait("tensor", sVE, ve_const)
                        if tt >= 1:
                            p.wait("tensor", sVE, m_xv)

                        def t_tr(te):
                            r = None
                            for hc in range(8):
                                r = te.transpose(
                                    out=ptrF[:, 128 * hc:128 * (hc + 1)],
                                    in_=xlnb2[:, 128 * hc:128 * (hc + 1)],
                                    identity=ident)
                            return r
                        m_trf = p.inc("tensor", t_tr, sTE)
                        p.wait("vector", sTE, m_trf)
                        m_xv = p.inc("vector", lambda ve, tt=tt: ve.tensor_copy(
                            out=xlnT[:, :, 128 * tt:128 * (tt + 1)],
                            in_=ptrF.rearrange("p (c f) -> p c f", c=8)), sVE)
                        p.wait("tensor", sVE, m_xv)
                        p.wait("tensor", sQ, m_gw)
                        if tt >= 1:
                            p.wait("tensor", sAE, m_sg)

                        def t_lg(te, tt=tt):
                            r = None
                            for hc in range(8):
                                r = te.matmul(
                                    out=pL, lhsT=xlnT[:, hc, 128 * tt:128 * (tt + 1)],
                                    rhs=gwS[:, hc], start=(hc == 0), stop=(hc == 7))
                            return r
                        m_lg = p.inc("tensor", t_lg, sTE)
                        p.wait("scalar", sTE, m_lg)
                        if tt >= 1:
                            p.wait("scalar", sVE, m_rtv)
                        m_sg = p.inc("scalar", lambda ae: ae.activation(
                            out=rt[:, 0:E], in_=pL, func=AF.Sigmoid), sAE)
                        p.wait("vector", sAE, m_sg)

                        def v_route(ve, tt=tt):
                            S0 = rt[:, 0:E]
                            SFC = rt[:, E:2 * E]
                            GM = rt[:, 2 * E:2 * E + 8]
                            GK = rt[:, 2 * E + 8:2 * E + 16]
                            MK = rt[:, 3 * E:4 * E]
                            SEL = rt[:, 4 * E:5 * E]
                            SCR = rt[:, 5 * E:6 * E]
                            vedge(ve, ve.tensor_tensor(out=SFC, in0=S0, in1=esc_s, op=ALU.add))
                            vedge(ve, ve.reduce_max(out=GM,
                                          in_=SFC.rearrange("p (g k) -> p g k", k=4),
                                          axis=AX))
                            vedge(ve, ve.tensor_copy(out=GK, in_=GM))
                            for _ in range(3):
                                vedge(ve, ve.reduce_max(out=mc[:, 0:1], in_=GK, axis=AX))
                                vedge(ve, ve.tensor_scalar(out=SCR[:, 0:8], in0=GK,
                                                 scalar1=mc[:, 0:1], scalar2=None,
                                                 op0=ALU.is_ge))
                                vedge(ve, ve.scalar_tensor_tensor(out=GK, in0=SCR[:, 0:8],
                                                        scalar=-BIG, in1=GK,
                                                        op0=ALU.mult, op1=ALU.add))
                            vedge(ve, ve.reduce_max(out=mc[:, 0:1], in_=GK, axis=AX))
                            vedge(ve, ve.tensor_scalar(out=GK, in0=GM, scalar1=mc[:, 0:1],
                                             scalar2=None, op0=ALU.is_ge))
                            for k4 in range(4):
                                vedge(ve, ve.tensor_tensor(
                                    out=MK.rearrange("p (g k) -> p g k", k=4)
                                    [:, :, k4:k4 + 1],
                                    in0=SFC.rearrange("p (g k) -> p g k", k=4)
                                    [:, :, k4:k4 + 1],
                                    in1=GK.rearrange("p (g o) -> p g o", o=1),
                                    op=ALU.mult))
                            vedge(ve, ve.tensor_copy(out=SEL, in_=MK))
                            for _ in range(5):
                                vedge(ve, ve.reduce_max(out=mc[:, 0:1], in_=SEL, axis=AX))
                                vedge(ve, ve.tensor_scalar(out=SCR, in0=SEL,
                                                 scalar1=mc[:, 0:1], scalar2=None,
                                                 op0=ALU.is_ge))
                                vedge(ve, ve.scalar_tensor_tensor(out=SEL, in0=SCR,
                                                        scalar=-BIG, in1=SEL,
                                                        op0=ALU.mult, op1=ALU.add))
                            vedge(ve, ve.reduce_max(out=mc[:, 0:1], in_=SEL, axis=AX))
                            vedge(ve, ve.tensor_scalar(out=SEL, in0=MK, scalar1=mc[:, 0:1],
                                             scalar2=None, op0=ALU.is_ge))
                            vedge(ve, ve.tensor_tensor(out=SEL, in0=S0, in1=SEL, op=ALU.mult))
                            vedge(ve, ve.reduce_sum(out=mc[:, 0:1], in_=SEL, axis=AX))
                            vedge(ve, ve.tensor_scalar(out=mc[:, 0:1], in0=mc[:, 0:1],
                                             scalar1=1e-20, scalar2=None, op0=ALU.add))
                            vedge(ve, ve.reciprocal(out=mc[:, 1:2], in_=mc[:, 0:1]))
                            return ve.tensor_scalar(out=cst[:, tt], in0=SEL,
                                                    scalar1=mc[:, 1:2], scalar2=2.5,
                                                    op0=ALU.mult, op1=ALU.mult)
                        m_rtv = p.inc("vector", v_route, sVE)
                    phase_barrier()
                    p.flush()
                xln_done = sVE.n
                rt_done = m_rtv

                pGU = ef(nc.psum_tensor(f"pGU{tag}", [128, 2, 2, 512], F32)).ap()
                pDN = ef(nc.psum_tensor(f"pDN{tag}", [128, 2, 1024], F32)).ap()
                sE = sbf(f"sE{tag}", [128, 8, 512], BF16)
                sgb = sbf(f"sgb{tag}", [128, 2, 512], BF16)
                gu_marks, x_marks = {}, {}
                bank_state = {"gu": [0, 0], "dn": [0, 0], "dn_te": 0, "sgb": [0, 0]}

                def emit_unit(n_ic, gate_tile, up_tile, dn_tile, ccol_fn, w_mark, ut):
                    for t5 in range(4):
                        dn_te_group = bank_state["dn_te"]
                        for j in range(n_ic):
                            bgu = j % 2
                            p.wait("tensor", sVE, bank_state["gu"][bgu])
                            p.wait("tensor", sQ, w_mark)
                            p.wait("tensor", sVE, xln_done)

                            def t_gu(te, t5=t5, j=j, bgu=bgu):
                                for hc in range(8):
                                    te.matmul(out=pGU[:, bgu, 0],
                                              lhsT=gate_tile(hc, j),
                                              rhs=xlnT[:, hc, 512 * t5:512 * (t5 + 1)],
                                              start=(hc == 0), stop=(hc == 7))
                                r = None
                                for hc in range(8):
                                    r = te.matmul(
                                        out=pGU[:, bgu, 1],
                                        lhsT=up_tile(hc, j),
                                        rhs=xlnT[:, hc, 512 * t5:512 * (t5 + 1)],
                                        start=(hc == 0), stop=(hc == 7))
                                return r
                            m_gu = p.inc("tensor", t_gu, sTE)
                            p.wait("scalar", sTE, m_gu)
                            p.wait("scalar", sVE, bank_state["sgb"][bgu])
                            m_si = p.inc("scalar", lambda ae, bgu=bgu: ae.activation(
                                out=sgb[:, bgu], in_=pGU[:, bgu, 0], func=AF.Silu),
                                sAE)
                            p.wait("vector", sAE, m_si)
                            p.wait("vector", sTE, m_gu)
                            p.wait("vector", sTE, dn_te_group)
                            mark = p.inc(
                                "vector", lambda ve, j=j, bgu=bgu: ve.tensor_tensor(
                                    out=sE[:, j, :], in0=sgb[:, bgu],
                                    in1=pGU[:, bgu, 1], op=ALU.mult), sVE)
                            gu_marks[(ut, t5, j)] = mark
                            bank_state["gu"][bgu] = mark
                            bank_state["sgb"][bgu] = mark
                        p.wait("tensor", sVE, gu_marks[(ut, t5, n_ic - 1)])
                        for ts in range(4):
                            tile = 4 * t5 + ts
                            bdn = ts % 2
                            p.wait("tensor", sVE, bank_state["dn"][bdn])

                            def t_dn(te, ts=ts, bdn=bdn, n_ic=n_ic):
                                r = None
                                for oh in range(2):
                                    for ic in range(n_ic):
                                        r = te.matmul(
                                            out=pDN[:, bdn, 512 * oh:512 * (oh + 1)],
                                            lhsT=sE[:, ic, 128 * ts:128 * (ts + 1)],
                                            rhs=dn_tile(ic)[:, 512 * oh:512 * (oh + 1)],
                                            start=(ic == 0), stop=(ic == n_ic - 1))
                                return r
                            m_dn = p.inc("tensor", t_dn, sTE)
                            bank_state["dn_te"] = m_dn
                            p.wait("vector", sTE, m_dn)
                            p.wait("vector", sVE, rt_done)
                            ccol = ccol_fn(tile)
                            if ccol is None:
                                mark = p.inc(
                                    "vector",
                                    lambda ve, tile=tile, bdn=bdn: ve.tensor_tensor(
                                        out=x_res[:, tile], in0=x_res[:, tile],
                                        in1=pDN[:, bdn], op=ALU.add), sVE)
                            else:
                                mark = p.inc(
                                    "vector",
                                    lambda ve, tile=tile, bdn=bdn, ccol=ccol:
                                    ve.scalar_tensor_tensor(
                                        out=x_res[:, tile], in0=pDN[:, bdn],
                                        scalar=ccol, in1=x_res[:, tile],
                                        op0=ALU.mult, op1=ALU.add), sVE)
                            x_marks[(ut, tile)] = mark
                            bank_state["dn"][bdn] = mark

                with ExitStack() as psh:
                    shW = psh.enter_context(
                        nc.sbuf_tensor(f"shW{tag}", [128, 3, 8, 1024], BF16)).ap()
                    p.wait("sync", sG, cast_marks["sdT"])
                    m_shw = 0
                    for wi, wsrc in ((0, sgT_b), (1, suT_b), (2, sdT_b)):
                        m_shw = p.inc("sync", lambda sy, wi=wi, wsrc=wsrc: sy.dma_start(
                            out=shW[:, wi],
                            in_=wsrc.rearrange("(c p) f -> p c f", p=128)), sQ, 16)
                    emit_unit(
                        8,
                        lambda hc, j: shW[:, 0, hc, 128 * j:128 * (j + 1)],
                        lambda hc, j: shW[:, 1, hc, 128 * j:128 * (j + 1)],
                        lambda ic: shW[:, 2, ic],
                        lambda tile: None, m_shw, "sh")
                    phase_barrier()
                    p.flush()

                guS = sbf(f"guS{tag}", [128, 2, 8, 1024], BF16)
                dnS = sbf(f"dnS{tag}", [128, 2, 4, 1024], BF16)
                for e_ in range(E):
                    bw = e_ % 2
                    if e_ >= 2:
                        p.wait("sync", sVE, x_marks[(e_ - 2, NT - 1)])
                    p.wait("sync", sG, cast_marks[f"dn{e_}"])
                    p.inc("sync", lambda sy, e_=e_, bw=bw: sy.dma_start(
                        out=guS[:, bw],
                        in_=guT_b[e_].rearrange("(c p) f -> p c f", p=128)), sQ, 16)
                    m_ew = p.inc("sync", lambda sy, e_=e_, bw=bw: sy.dma_start(
                        out=dnS[:, bw],
                        in_=dnT_b[e_].rearrange("(c p) f -> p c f", p=128)), sQ, 16)
                    emit_unit(
                        4,
                        lambda hc, j, bw=bw: guS[:, bw, hc, 128 * j:128 * (j + 1)],
                        lambda hc, j, bw=bw: guS[:, bw, hc,
                                                 512 + 128 * j:512 + 128 * (j + 1)],
                        lambda ic, bw=bw: dnS[:, bw, ic],
                        lambda tile, e_=e_: cst[:, tile, e_:e_ + 1], m_ew, e_)
                phase_barrier()
                p.flush()

        if upto >= 2:
            emit_ffn("F1")
            if tap_phase == "p2":
                emit_store(p, x_res, tap, sO, sVE)
                p.flush()
                return nc

        if upto >= 3:
            phase_barrier()
            p.serial_te = ((sVE, "vector"), (sAE, "scalar"), (sTE, "tensor"))
            if tap_phase == "p3z":
                emit_store(p, x_res, tap, sO, sVE)
                p.flush()
                return nc
            with ExitStack() as p3:
                e3 = p3.enter_context
                sb3 = lambda n, s, d_: e3(nc.sbuf_tensor(n, s, d_)).ap()
                xlnT3 = sb3("xlnT3", [128, 8, 2052], BF16)
                xlnb3 = sb3("xlnb3", [128, 1024], BF16)
                oloc = sb3("oloc", [128, NT, 256], BF16)
                qhat = sb3("qhat", [128, 2, T_OWN], BF16)
                sq256 = sb3("sq256", [128, 256], BF16)
                haloA = sb3("haloA", [128, 8, 24], BF16)
                hown = sb3("hown", [128, 24], BF16)
                contribH = sb3("contribH", [128, 1024], BF16)
                ptrB = e3(nc.psum_tensor("ptrB", [128, 1024], F32)).ap()
                pPr = e3(nc.psum_tensor("pPr", [128, 2, 512], F32)).ap()
                pX = e3(nc.psum_tensor("pX", [128, 2, 512], F32)).ap()
                pY = e3(nc.psum_tensor("pY", [128, 2, 256], F32)).ap()
                pZ = e3(nc.psum_tensor("pZ", [128, 2, 256], F32)).ap()

                m_xl3 = 0
                for tt in range(NT):
                    m_ln = emit_ln(x_res[:, tt], xlnb3, tt)
                    p.wait("tensor", sVE, m_ln)
                    if tt >= 1:
                        p.wait("tensor", sVE, m_xl3)

                    def t_tr3(te):
                        r = None
                        for hc in range(8):
                            r = te.matmul(out=ptrB[:, 128 * hc:128 * (hc + 1)],
                                          lhsT=xlnb3[:, 128 * hc:128 * (hc + 1)],
                                          rhs=ident, start=True, stop=True)
                        return r
                    m_tr3 = p.inc("tensor", t_tr3, sTE)
                    p.wait("vector", sTE, m_tr3)
                    m_xl3 = p.inc("vector", lambda ve, tt=tt: ve.tensor_copy(
                        out=xlnT3[:, :, 3 + 128 * tt:3 + 128 * (tt + 1)],
                        in_=ptrB.rearrange("p (c f) -> p c f", c=8)), sVE)

                if tap_phase == "p3y":
                    emit_store(p, x_res, tap, sO, sVE)
                    p.flush()
                    return nc

                p.wait("vector", sVE, m_xl3)
                m_hcp = p.inc("vector", lambda ve: ve.tensor_copy(
                    out=hown.rearrange("p (a b) -> p a b", b=3),
                    in_=xlnT3[:, :, 2048:2051]), sVE)
                p.wait("gpsimd", sVE, m_hcp)
                m_hout = p.inc("gpsimd", lambda gp: gp.dma_start(
                    out=halo_in, in_=hown), sG, 16)
                p.wait("gpsimd", sG, m_hout)
                if tap_phase == "p3nc":
                    for r_ in range(8):
                        p.inc("gpsimd", lambda gp, r_=r_: gp.dma_start(
                            out=halo_out[128 * r_:128 * (r_ + 1), :],
                            in_=halo_in), sG, 16)
                    m_cc1 = sCC.n
                else:
                    m_cc1 = p.inc("gpsimd", lambda gp: gp.collective_compute(
                        "AllGather", ALU.bypass,
                        replica_groups=[[0, 1, 2, 3, 4, 5, 6, 7]],
                        ins=[halo_in.opt()], outs=[halo_out.opt()]), sCC, 1)
                p.wait("gpsimd", sCC, m_cc1)
                for r_ in range(8):
                    m_hin = p.inc("gpsimd", lambda gp, r_=r_: gp.dma_start(
                        out=haloA[:, r_, :],
                        in_=halo_out[128 * r_:128 * (r_ + 1), :]), sG, 16)
                p.wait("vector", sG, m_hin)

                def v_halo(ve):
                    vedge(ve, ve.tensor_scalar(out=hown, in0=haloA[:, 0],
                                               scalar1=hsel[:, 0:1], scalar2=None,
                                               op0=ALU.mult))
                    for r_ in range(1, 8):
                        vedge(ve, ve.scalar_tensor_tensor(
                            out=hown, in0=haloA[:, r_],
                            scalar=hsel[:, r_:r_ + 1], in1=hown,
                            op0=ALU.mult, op1=ALU.add))
                    return ve.tensor_copy(
                        out=xlnT3[:, :, 0:3],
                        in_=hown.rearrange("p (a b) -> p a b", b=3))
                m_halo = p.inc("vector", v_halo, sVE)
                if tap_phase in ("p3a", "p3nc"):
                    p.wait("gpsimd", sVE, m_halo)
                    mwg = p.inc("gpsimd", lambda gp: gp.dma_start(
                        out=dWg, in_=guT_b[0]), sG, 16)
                    p.wait("gpsimd", sG, mwg)
                    emit_store(p, x_res, tap, sO, sVE)
                    emit_store(p, x_res, tap2, sO, sVE)
                    p.flush()
                    return nc

                NCH_ = 16
                st_contrib = [0] * H_B
                with ExitStack() as p3h:
                    e3h = p3h.enter_context
                    sb3h = lambda n, s, d_: e3h(nc.sbuf_tensor(n, s, d_)).ap()
                    qf = sb3h("qf", [128, 2, 2048], BF16)
                    kf = sb3h("kf", [128, 2, 2048], BF16)
                    q_t = sb3h("q_t", [128, NT, 256], BF16)
                    k_t = sb3h("k_t", [128, NT, 256], BF16)
                    v_t = sb3h("v_t", [128, NT, 256], BF16)
                    lin = sb3h("lin", [128, 2051], F32)
                    ybuf = sb3h("ybuf", [128, 2048], F32)
                    sfm = sb3h("sfm", [128, 2048], BF16)
                    wsl = sb3h("wsl", [128, 3, 8, 256], BF16)
                    S_f = sb3h("S_f", [128, 2, 256], F32)
                    S_bp = sb3h("S_bp", [128, 2, 256], BF16)
                    S_bn = sb3h("S_bn", [128, 2, 256], BF16)
                    P_f = sb3h("P_f", [128, 2, 256], F32)
                    P_b = sb3h("P_b", [128, 2, 256], BF16)
                    Tt = sb3h("Tt", [128, 128], BF16)
                    Lp = sb3h("Lp", [128, 6, 128], BF16)
                    Up = sb3h("Up", [128, 6, 128], BF16)
                    Lb = sb3h("Lb", [128, 128], BF16)
                    Ub = sb3h("Ub", [128, 128], BF16)
                    Mt = sb3h("Mt", [128, 128], BF16)
                    TKb = sb3h("TKb", [128, 256], BF16)
                    TKtb = sb3h("TKtb", [128, 2, 128], BF16)
                    Ulb = sb3h("Ulb", [128, 256], BF16)
                    Qet = sb3h("Qet", [128, 2, 128], BF16)
                    Wb = sb3h("Wb", [128, 256], BF16)

                    head_te_done = 0
                    for h_i, h in enumerate(P3_HEADS if P3_HEADS is not None else range(H_B)):
                        p.wait("sync", sG, cast_marks["bWv"])
                        p.wait("sync", sTE, head_te_done)
                        p.wait("sync", sVE, sVE.n)
                        m_wsl = 0
                        for wi, wsrc in ((0, bWq_b), (1, bWk_b), (2, bWv_b)):
                            m_wsl = p.inc(
                                "sync", lambda sy, wi=wi, wsrc=wsrc, h=h:
                                sy.dma_start(
                                    out=wsl[:, wi],
                                    in_=wsrc.rearrange("(c p) f -> p c f", p=128)
                                    [:, :, 256 * h:256 * (h + 1)]), sQ, 16)

                        m_tok = 0
                        if h_i == 0:
                            prev_pj_evs = [0] * 5
                        _wis = range(3)
                        if h_i > 0 and "prep" in P3_SKIP:
                            _wis = ()
                        if h_i > 0 and "wi0" in P3_PARTS:
                            _wis = (0,)
                        for wi in _wis:
                            fm = (qf, kf, None)[wi]
                            tok_dst = (q_t, k_t, v_t)[wi]
                            _kcs = range(2)
                            if h_i > 0 and "kc0" in P3_PARTS:
                                _kcs = (0,)
                            if h_i > 0 and "kc1" in P3_PARTS:
                                _kcs = (1,)
                            for kc in _kcs:
                                p.wait("tensor", sQ, m_wsl)
                                p.wait("tensor", sVE, m_halo)
                                pj_evs = [0] * 5
                                for t5 in (range(5) if (h_i == 0 or "proj" in P3_PARTS) else ()):
                                    bpr = t5 % 2
                                    lo = 512 * t5 if t5 < 4 else 2051 - 512
                                    w_ = 512
                                    p.wait("tensor", sVE, sVE.n)
                                    p.wait("tensor", sAE, sAE.n)
                                    if t5 >= 2:
                                        p.wait("tensor", sVE, pj_evs[t5 - 2])
                                    else:
                                        p.wait("tensor", sVE, prev_pj_evs[4 - t5])

                                    def t_pj(te, kc=kc, lo=lo, w_=w_, bpr=bpr, wi=wi):
                                        r = None
                                        for hc in range(8):
                                            r = te.matmul(
                                                out=pPr[:, bpr, 0:w_],
                                                lhsT=wsl[:, wi, hc,
                                                         128 * kc:128 * (kc + 1)],
                                                rhs=xlnT3[:, hc, lo:lo + w_],
                                                start=(hc == 0), stop=(hc == 7))
                                        return r
                                    m_pj = p.inc("tensor", t_pj, sTE)
                                    p.wait("vector", sTE, m_pj)
                                    if t5 == 0 and kc == 0 and wi == 0:
                                        p.wait("vector", sTE, head_te_done)
                                    pj_evs[t5] = p.inc(
                                        "vector",
                                        lambda ve, lo=lo, w_=w_, bpr=bpr:
                                        ve.tensor_copy(out=lin[:, lo:lo + w_],
                                                       in_=pPr[:, bpr, 0:w_]), sVE)

                                prev_pj_evs = pj_evs
                                do_conv = (h_i == 0 or "conv" in P3_PARTS)
                                cw = conv_s[:, wi, 2 * h + kc]
                                p.wait("vector", sVE, pj_evs[4])

                                def v_conv(ve, cw=cw):
                                    ve.tensor_scalar(out=ybuf, in0=lin[:, 0:2048],
                                                     scalar1=cw[:, 0:1], scalar2=None,
                                                     op0=ALU.mult)
                                    r = None
                                    for i_ in range(1, 4):
                                        r = ve.scalar_tensor_tensor(
                                            out=ybuf, in0=lin[:, i_:2048 + i_],
                                            scalar=cw[:, i_:i_ + 1], in1=ybuf,
                                            op0=ALU.mult, op1=ALU.add)
                                    return r
                                if do_conv:
                                    m_cv = p.inc("vector", v_conv, sVE)
                                if h_i == 0 or "sil" in P3_PARTS:
                                    p.wait("scalar", sVE, m_cv)
                                    m_sil = p.inc("scalar", lambda ae: ae.activation(
                                        out=sfm, in_=ybuf, func=AF.Silu), sAE)

                                for half8 in (range(2) if (h_i == 0 or "tok" in P3_PARTS) else ()):
                                    p.wait("tensor", sAE, m_sil)
                                    p.wait("tensor", sVE, m_tok)
                                    if wi == 0 and kc == 0 and half8 == 0:
                                        p.wait("tensor", sTE, head_te_done)

                                    def t_tt(te, half8=half8):
                                        r = None
                                        for t8 in range(8):
                                            o_ = 1024 * half8 + 128 * t8
                                            r = te.matmul(
                                                out=ptrB[:, 128 * t8:128 * (t8 + 1)],
                                                lhsT=sfm[:, o_:o_ + 128],
                                                rhs=ident, start=True, stop=True)
                                        return r
                                    m_ttr = p.inc("tensor", t_tt, sTE)
                                    p.wait("vector", sTE, m_ttr)

                                    def v_tok(ve, tok_dst=tok_dst, kc=kc,
                                              half8=half8):
                                        r = None
                                        for t8 in range(8):
                                            r = ve.tensor_copy(
                                                out=tok_dst[:, 8 * half8 + t8,
                                                            128 * kc:128 * (kc + 1)],
                                                in_=ptrB[:, 128 * t8:128 * (t8 + 1)])
                                        return r
                                    m_tok = p.inc("vector", v_tok, sVE)

                            if wi < 2 and (h_i == 0 or "nrm" in P3_PARTS):
                                for tt in range(NT):
                                    p.wait("vector", sVE, m_tok)

                                    def v_sq(ve, tok_dst=tok_dst, tt=tt):
                                        return ve.scalar_tensor_tensor(
                                            out=sq256, in0=tok_dst[:, tt], scalar=1.0,
                                            in1=tok_dst[:, tt], op0=ALU.mult,
                                            op1=ALU.mult,
                                            accum_out=mc[:, tt % 8:tt % 8 + 1])
                                    m_sq = p.inc("vector", v_sq, sVE)
                                    p.wait("vector", sVE, m_sq)
                                    m_sq = p.inc(
                                        "vector", lambda ve, tt=tt: ve.tensor_scalar(
                                            out=mc[:, tt % 8:tt % 8 + 1],
                                            in0=mc[:, tt % 8:tt % 8 + 1],
                                            scalar1=1e-6, scalar2=None,
                                            op0=ALU.add), sVE)
                                    p.wait("scalar", sVE, m_sq)
                                    m_sqs = p.inc(
                                        "scalar", lambda ae, tt=tt: ae.activation(
                                            out=mc[:, 8 + tt % 8:9 + tt % 8],
                                            in_=mc[:, tt % 8:tt % 8 + 1],
                                            func=AF.Sqrt), sAE)
                                    p.wait("vector", sAE, m_sqs)

                                    def v_nrm(ve, tok_dst=tok_dst, tt=tt, wi=wi):
                                        vedge(ve, ve.reciprocal(
                                            out=mc[:, tt % 8:tt % 8 + 1],
                                            in_=mc[:, 8 + tt % 8:9 + tt % 8]))
                                        s2 = (DH_B ** -0.5) if wi == 0 else 1.0
                                        return ve.tensor_scalar(
                                            out=tok_dst[:, tt], in0=tok_dst[:, tt],
                                            scalar1=mc[:, tt % 8:tt % 8 + 1],
                                            scalar2=s2, op0=ALU.mult, op1=ALU.mult)
                                    m_tok = p.inc("vector", v_nrm, sVE)
                                for kc in range(2):
                                    for half8 in range(2):
                                        p.wait("tensor", sVE, m_tok)

                                        def t_tb(te, tok_dst=tok_dst, kc=kc,
                                                 half8=half8):
                                            r = None
                                            for t8 in range(8):
                                                tt = 8 * half8 + t8
                                                r = te.matmul(
                                                    out=ptrB[:, 128 * t8:
                                                             128 * (t8 + 1)],
                                                    lhsT=tok_dst[:, tt, 128 * kc:
                                                                 128 * (kc + 1)],
                                                    rhs=ident, start=True,
                                                    stop=True)
                                            return r
                                        m_tbr = p.inc("tensor", t_tb, sTE)
                                        p.wait("vector", sTE, m_tbr)
                                        m_tok = p.inc(
                                            "vector",
                                            lambda ve, fm=fm, kc=kc, half8=half8:
                                            ve.tensor_copy(
                                                out=fm[:, kc, 1024 * half8:
                                                       1024 * (half8 + 1)],
                                                in_=ptrB), sVE)

                        def v_init(ve):
                            ve.memset(S_f, 0.0)
                            ve.memset(S_bp, 0.0)
                            ve.memset(S_bn, 0.0)
                            ve.memset(P_f, 0.0)
                            for a in range(2):
                                ve.tensor_copy(out=P_f[:, a, 128 * a:128 * (a + 1)],
                                               in_=ident)
                            return ve.tensor_copy(out=P_b, in_=P_f)
                        p.wait("vector", sVE, m_tok)
                        m_v = p.inc("vector", v_init, sVE)

                        for n_ in range(P3_NCH):
                            cs = slice(128 * n_, 128 * (n_ + 1))
                            p.wait("tensor", sVE, m_v)

                            def t_gram(te, cs=cs):
                                for a in range(2):
                                    te.matmul(out=pX[:, 0, 0:128], lhsT=kf[:, a, cs],
                                              rhs=kf[:, a, cs], start=(a == 0),
                                              stop=(a == 1))
                                r = None
                                for a in range(2):
                                    r = te.matmul(out=pX[:, 0, 128:256],
                                                  lhsT=kf[:, a, cs], rhs=qf[:, a, cs],
                                                  start=(a == 0), stop=(a == 1))
                                return r
                            m_t = p.inc("tensor", t_gram, sTE)
                            p.wait("vector", sTE, m_t)

                            def v_mask(ve):
                                ve.tensor_tensor(out=Lb, in0=pX[:, 0, 0:128],
                                                 in1=stril, op=ALU.mult)
                                ve.tensor_tensor(out=Ub, in0=pX[:, 0, 0:128],
                                                 in1=striu, op=ALU.mult)
                                return ve.tensor_tensor(out=Mt, in0=pX[:, 0, 128:256],
                                                        in1=triu_i, op=ALU.mult)
                            m_v = p.inc("vector", v_mask, sVE)

                            for lev in range(6):
                                p.wait("tensor", sVE, m_v)
                                pL_ = Lb if lev == 0 else Lp[:, lev - 1]
                                pU_ = Ub if lev == 0 else Up[:, lev - 1]

                                def t_pow(te, pL_=pL_, pU_=pU_):
                                    te.matmul(out=pX[:, 1, 0:128], lhsT=pU_, rhs=pL_,
                                              start=True, stop=True)
                                    return te.matmul(out=pX[:, 1, 128:256], lhsT=pL_,
                                                     rhs=pU_, start=True, stop=True)
                                m_t = p.inc("tensor", t_pow, sTE)
                                p.wait("vector", sTE, m_t)

                                def v_pow(ve, lev=lev):
                                    ve.tensor_copy(out=Lp[:, lev], in_=pX[:, 1, 0:128])
                                    return ve.tensor_copy(out=Up[:, lev],
                                                          in_=pX[:, 1, 128:256])
                                m_v = p.inc("vector", v_pow, sVE)

                            def v_t0(ve):
                                return ve.tensor_add(out=Tt, in0=ident, in1=Up[:, 5])
                            m_v = p.inc("vector", v_t0, sVE)
                            for k_ in (4, 3, 2, 1, 0, -1):
                                p.wait("tensor", sVE, m_v)
                                lhs_ = Lp[:, k_] if k_ >= 0 else Lb

                                def t_ch(te, lhs_=lhs_):
                                    return te.matmul(out=pX[:, 1, 0:128], lhsT=lhs_,
                                                     rhs=Tt, start=True, stop=True)
                                m_t = p.inc("tensor", t_ch, sTE)
                                p.wait("vector", sTE, m_t)
                                op_ = ALU.add if k_ >= 0 else ALU.subtract

                                def v_ch(ve, op_=op_):
                                    return ve.tensor_tensor(out=Tt, in0=Tt,
                                                            in1=pX[:, 1, 0:128],
                                                            op=op_)
                                m_v = p.inc("vector", v_ch, sVE)

                            p.wait("tensor", sVE, m_v)

                            def t_tk(te, n_=n_):
                                te.matmul(out=pY[:, 0], lhsT=Tt, rhs=k_t[:, n_],
                                          start=True, stop=True)
                                te.matmul(out=pY[:, 1, 0:128],
                                          lhsT=k_t[:, n_, 0:128], rhs=Tt,
                                          start=True, stop=True)
                                return te.matmul(out=pY[:, 1, 128:256],
                                                 lhsT=k_t[:, n_, 128:256], rhs=Tt,
                                                 start=True, stop=True)
                            m_t = p.inc("tensor", t_tk, sTE)
                            p.wait("vector", sTE, m_t)

                            def v_tk(ve):
                                ve.tensor_copy(out=TKb, in_=pY[:, 0])
                                ve.tensor_copy(out=TKtb[:, 0], in_=pY[:, 1, 0:128])
                                return ve.tensor_copy(out=TKtb[:, 1],
                                                      in_=pY[:, 1, 128:256])
                            m_v = p.inc("vector", v_tk, sVE)

                            p.wait("tensor", sVE, m_v)

                            def t_ul(te, n_=n_):
                                te.matmul(out=pZ[:, 0], lhsT=Tt, rhs=v_t[:, n_],
                                          start=True, stop=False)
                                te.matmul(out=pZ[:, 0], lhsT=TKtb[:, 0],
                                          rhs=S_bn[:, 0], start=False, stop=False)
                                return te.matmul(out=pZ[:, 0], lhsT=TKtb[:, 1],
                                                 rhs=S_bn[:, 1], start=False,
                                                 stop=True)
                            m_t = p.inc("tensor", t_ul, sTE)
                            p.wait("vector", sTE, m_t)
                            m_v = p.inc("vector", lambda ve: ve.tensor_copy(
                                out=Ulb, in_=pZ[:, 0]), sVE)

                            p.wait("tensor", sVE, m_v)

                            def t_ol(te, cs=cs):
                                te.matmul(out=pZ[:, 1], lhsT=qf[:, 0, cs],
                                          rhs=S_bp[:, 0], start=True, stop=False)
                                te.matmul(out=pZ[:, 1], lhsT=qf[:, 1, cs],
                                          rhs=S_bp[:, 1], start=False, stop=False)
                                return te.matmul(out=pZ[:, 1], lhsT=Mt, rhs=Ulb,
                                                 start=False, stop=True)
                            m_t = p.inc("tensor", t_ol, sTE)
                            p.wait("vector", sTE, m_t)
                            m_v = p.inc("vector", lambda ve, n_=n_: ve.tensor_copy(
                                out=oloc[:, n_], in_=pZ[:, 1]), sVE)

                            p.wait("tensor", sVE, m_v)

                            def t_qe(te):
                                te.matmul(out=pX[:, 0, 0:128], lhsT=TKb[:, 0:128],
                                          rhs=Mt, start=True, stop=True)
                                return te.matmul(out=pX[:, 0, 128:256],
                                                 lhsT=TKb[:, 128:256], rhs=Mt,
                                                 start=True, stop=True)
                            m_t = p.inc("tensor", t_qe, sTE)
                            p.wait("vector", sTE, m_t)

                            def v_qe(ve, cs=cs):
                                r = None
                                for b_ in range(2):
                                    r = ve.tensor_sub(
                                        out=Qet[:, b_], in0=qf[:, b_, cs],
                                        in1=pX[:, 0, 128 * b_:128 * (b_ + 1)])
                                return r
                            m_v = p.inc("vector", v_qe, sVE)
                            p.wait("tensor", sVE, m_v)

                            def t_qh(te):
                                r = None
                                for b_ in range(2):
                                    for a in range(2):
                                        r = te.matmul(
                                            out=pY[:, 1, 128 * b_:128 * (b_ + 1)],
                                            lhsT=P_b[:, a, 128 * b_:128 * (b_ + 1)],
                                            rhs=Qet[:, a],
                                            start=(a == 0), stop=(a == 1))
                                return r
                            m_t = p.inc("tensor", t_qh, sTE)
                            p.wait("vector", sTE, m_t)

                            def v_qh(ve, cs=cs):
                                r = None
                                for b_ in range(2):
                                    r = ve.tensor_copy(
                                        out=qhat[:, b_, cs],
                                        in_=pY[:, 1, 128 * b_:128 * (b_ + 1)])
                                return r
                            m_v = p.inc("vector", v_qh, sVE)

                            p.wait("tensor", sVE, m_v)

                            def t_su(te, n_=n_):
                                r = None
                                for a in range(2):
                                    r = te.matmul(
                                        out=pZ[:, a],
                                        lhsT=k_t[:, n_, 128 * a:128 * (a + 1)],
                                        rhs=Ulb, start=True, stop=True)
                                return r
                            m_t = p.inc("tensor", t_su, sTE)
                            p.wait("vector", sTE, m_t)

                            def v_su(ve):
                                r = None
                                for a in range(2):
                                    vedge(ve, ve.tensor_add(out=S_f[:, a],
                                                            in0=S_f[:, a],
                                                            in1=pZ[:, a]))
                                    ve.tensor_copy(out=S_bp[:, a], in_=S_f[:, a])
                                    r = ve.tensor_scalar(out=S_bn[:, a],
                                                         in0=S_f[:, a],
                                                         scalar1=-1.0, scalar2=None,
                                                         op0=ALU.mult)
                                return r
                            m_v = p.inc("vector", v_su, sVE)

                            p.wait("tensor", sVE, m_v)

                            def t_w(te):
                                te.matmul(out=pY[:, 0], lhsT=TKtb[:, 0],
                                          rhs=P_b[:, 0], start=True, stop=False)
                                return te.matmul(out=pY[:, 0], lhsT=TKtb[:, 1],
                                                 rhs=P_b[:, 1], start=False,
                                                 stop=True)
                            m_t = p.inc("tensor", t_w, sTE)
                            p.wait("vector", sTE, m_t)
                            m_v = p.inc("vector", lambda ve: ve.tensor_copy(
                                out=Wb, in_=pY[:, 0]), sVE)
                            p.wait("tensor", sVE, m_v)

                            def t_pw(te, n_=n_):
                                r = None
                                for a in range(2):
                                    r = te.matmul(
                                        out=pZ[:, a],
                                        lhsT=k_t[:, n_, 128 * a:128 * (a + 1)],
                                        rhs=Wb, start=True, stop=True)
                                return r
                            m_t = p.inc("tensor", t_pw, sTE)
                            p.wait("vector", sTE, m_t)

                            def v_pu(ve):
                                r = None
                                for a in range(2):
                                    vedge(ve, ve.tensor_sub(out=P_f[:, a],
                                                            in0=P_f[:, a],
                                                            in1=pZ[:, a]))
                                    r = ve.tensor_copy(out=P_b[:, a], in_=P_f[:, a])
                                return r
                            m_v = p.inc("vector", v_pu, sVE)

                        if "tail" in P3_SKIP and h_i > 0:
                            head_te_done = sTE.n
                            continue
                        p.wait("sync", sVE, m_v)
                        p.inc("sync", lambda sy, h=h: sy.dma_start(
                            out=olocal_d[:, :, 256 * h:256 * (h + 1)], in_=oloc),
                            sQ, 16)
                        p.inc("sync", lambda sy, h=h: sy.dma_start(
                            out=qhat_d[h].rearrange("b p t -> p b t"), in_=qhat),
                            sQ, 16)
                        p.wait("tensor", sVE, m_v)

                        def t_ct(te):
                            r = None
                            for a_ in range(2):
                                for b_ in range(2):
                                    r = te.matmul(
                                        out=ptrB[:, 256 * a_ + 128 * b_:
                                                 256 * a_ + 128 * (b_ + 1)],
                                        lhsT=P_b[:, b_, 128 * a_:128 * (a_ + 1)],
                                        rhs=ident, start=True, stop=True)
                            return r
                        m_t = p.inc("tensor", t_ct, sTE)
                        p.wait("vector", sTE, m_t)

                        def v_ct(ve):
                            ve.tensor_copy(out=contribH[:, 0:512], in_=ptrB[:, 0:512])
                            r = None
                            for a in range(2):
                                r = ve.tensor_copy(
                                    out=contribH[:, 512 + 256 * a:
                                                 512 + 256 * (a + 1)],
                                    in_=S_bp[:, a])
                            return r
                        m_ct = p.inc("vector", v_ct, sVE)
                        p.wait("sync", sVE, m_ct)
                        st_contrib[h] = p.inc("sync", lambda sy, h=h: sy.dma_start(
                            out=st_in[:, 1024 * h:1024 * (h + 1)], in_=contribH),
                            sQ, 16)
                        head_te_done = sTE.n

                if tap_phase == "p3b":
                    emit_store(p, x_res, tap, sO, sVE)
                    p.flush()
                    return nc

                p.wait("gpsimd", sQ, st_contrib[H_B - 1])
                m_cc2 = p.inc("gpsimd", lambda gp: gp.collective_compute(
                    "AllGather", ALU.bypass,
                    replica_groups=[[0, 1, 2, 3, 4, 5, 6, 7]],
                    ins=[st_in.opt()], outs=[st_out.opt()]), sCC, 1)
                p.wait("gpsimd", sCC, m_cc2)

                phase_barrier()
                with ExitStack() as p3c:
                    e3c = p3c.enter_context
                    sb3c = lambda n, s, d_: e3c(nc.sbuf_tensor(n, s, d_)).ap()
                    stb = sb3c("stb", [128, 4096], BF16)
                    SinO = sb3c("SinO", [128, H_B, 2, 256], F32)
                    SinOb = sb3c("SinOb", [128, H_B, 2, 256], BF16)
                    Scur = sb3c("Scur", [128, H_B, 2, 256], BF16)
                    Snxt = sb3c("Snxt", [128, 2, 256], F32)
                    OTB = sb3c("OTB", [128, 8, 2048], BF16)
                    WoB = sb3c("WoB", [128, 8, 1024], BF16)

                    def v_c0(ve):
                        ve.memset(SinO, 0.0)
                        return ve.memset(Scur, 0.0)
                    m_v = p.inc("vector", v_c0, sVE)

                    m_stb = 0
                    m_hv = m_v
                    for chain in range(2):
                      for r_ in range(3):
                        gr = 4 * chain + r_
                        scol = 3 * chain + r_
                        p.wait("gpsimd", sVE, m_hv)
                        m_stb = p.inc("gpsimd", lambda gp, gr=gr: gp.dma_start(
                            out=stb, in_=st_out[128 * gr:128 * (gr + 1), :]), sG, 16)
                        p.wait("vector", sG, m_stb)
                        p.wait("tensor", sG, m_stb)
                        for h in range(H_B):
                            if r_ == 0:
                                def v_first(ve, h=h):
                                    r = None
                                    for b_ in range(2):
                                        r = ve.tensor_copy(
                                            out=Snxt[:, b_],
                                            in_=stb[:, 1024 * h + 512 + 256 * b_:
                                                    1024 * h + 512 + 256 * (b_ + 1)])
                                    return r
                                m_hv = p.inc("vector", v_first, sVE)
                            else:
                                def t_cp(te, h=h):
                                    r = None
                                    for b_ in range(2):
                                        for a_ in range(2):
                                            r = te.matmul(
                                                out=pY[:, b_],
                                                lhsT=stb[:, 1024 * h + 256 * a_
                                                         + 128 * b_:
                                                         1024 * h + 256 * a_
                                                         + 128 * (b_ + 1)],
                                                rhs=Scur[:, h, a_],
                                                start=(a_ == 0), stop=(a_ == 1))
                                    return r
                                p.wait("tensor", sVE, m_hv)
                                m_t = p.inc("tensor", t_cp, sTE)
                                p.wait("vector", sTE, m_t)

                                def v_nx(ve, h=h):
                                    r = None
                                    for b_ in range(2):
                                        r = ve.tensor_tensor(
                                            out=Snxt[:, b_], in0=pY[:, b_],
                                            in1=stb[:, 1024 * h + 512 + 256 * b_:
                                                    1024 * h + 512 + 256 * (b_ + 1)],
                                            op=ALU.add)
                                    return r
                                m_hv = p.inc("vector", v_nx, sVE)

                            p.wait("vector", sVE, m_hv)

                            def v_sel(ve, h=h, scol=scol):
                                r = None
                                for b_ in range(2):
                                    ve.scalar_tensor_tensor(
                                        out=SinO[:, h, b_], in0=Snxt[:, b_],
                                        scalar=ssel[:, scol:scol + 1],
                                        in1=SinO[:, h, b_],
                                        op0=ALU.mult, op1=ALU.add)
                                    r = ve.tensor_copy(out=Scur[:, h, b_],
                                                       in_=Snxt[:, b_])
                                return r
                            m_v = p.inc("vector", v_sel, sVE)
                    m_v = p.inc("vector", lambda ve: ve.tensor_copy(
                        out=SinOb, in_=SinO), sVE)

                    if tap_phase == "p3c":
                        emit_store(p, x_res, tap, sO, sVE)
                        p.flush()
                        return nc

                    p.wait("sync", sG, cast_marks["bWo"])
                    m_wob = p.inc("sync", lambda sy: sy.dma_start(
                        out=WoB, in_=bWo_b.rearrange("(c p) f -> p c f", p=128)),
                        sQ, 16)

                    m_otb = 0
                    for h in range(H_B):
                        p.wait("sync", sVE, m_v if h == 0 else m_otb)
                        m_ol = p.inc("sync", lambda sy, h=h: sy.dma_start(
                            out=oloc, in_=olocal_d[:, :, 256 * h:256 * (h + 1)]),
                            sQ, 16)
                        m_qh = p.inc("sync", lambda sy, h=h: sy.dma_start(
                            out=qhat, in_=qhat_d[h].rearrange("b p t -> p b t")),
                            sQ, 16)
                        p.wait("tensor", sQ, m_qh)
                        p.wait("vector", sQ, m_ol)
                        corr_ms = [0] * NCH_
                        for n_ in range(NCH_):
                            cs = slice(128 * n_, 128 * (n_ + 1))
                            if n_ >= 2:
                                p.wait("tensor", sVE, corr_ms[n_ - 2])
                            p.wait("tensor", sVE, m_v)

                            def t_cr(te, cs=cs, h=h, n_=n_):
                                r = None
                                for b_ in range(2):
                                    r = te.matmul(out=pZ[:, n_ % 2],
                                                  lhsT=qhat[:, b_, cs],
                                                  rhs=SinOb[:, h, b_],
                                                  start=(b_ == 0), stop=(b_ == 1))
                                return r
                            m_t = p.inc("tensor", t_cr, sTE)
                            p.wait("vector", sTE, m_t)

                            def v_cr(ve, n_=n_):
                                return ve.tensor_add(out=oloc[:, n_],
                                                     in0=oloc[:, n_],
                                                     in1=pZ[:, n_ % 2])
                            m_cr = p.inc("vector", v_cr, sVE)
                            p.wait("vector", sVE, m_cr)

                            def v_sq2(ve, n_=n_):
                                return ve.scalar_tensor_tensor(
                                    out=sq256, in0=oloc[:, n_], scalar=1.0,
                                    in1=oloc[:, n_], op0=ALU.mult, op1=ALU.mult,
                                    accum_out=mc[:, n_ % 8:n_ % 8 + 1])
                            m_sq = p.inc("vector", v_sq2, sVE)
                            p.wait("vector", sVE, m_sq)
                            m_sq = p.inc(
                                "vector", lambda ve, n_=n_: ve.tensor_scalar(
                                    out=mc[:, n_ % 8:n_ % 8 + 1],
                                    in0=mc[:, n_ % 8:n_ % 8 + 1],
                                    scalar1=1.0 / 256.0, scalar2=1e-5,
                                    op0=ALU.mult, op1=ALU.add), sVE)
                            p.wait("scalar", sVE, m_sq)
                            m_sqs = p.inc("scalar", lambda ae, n_=n_: ae.activation(
                                out=mc[:, 8 + n_ % 8:9 + n_ % 8],
                                in_=mc[:, n_ % 8:n_ % 8 + 1], func=AF.Sqrt), sAE)
                            p.wait("vector", sAE, m_sqs)

                            def v_rms(ve, n_=n_):
                                vedge(ve, ve.reciprocal(
                                    out=mc[:, n_ % 8:n_ % 8 + 1],
                                    in_=mc[:, 8 + n_ % 8:9 + n_ % 8]))
                                return ve.scalar_tensor_tensor(
                                    out=oloc[:, n_], in0=oloc[:, n_],
                                    scalar=mc[:, n_ % 8:n_ % 8 + 1], in1=onorm_s,
                                    op0=ALU.mult, op1=ALU.mult)
                            m_rms = p.inc("vector", v_rms, sVE)
                            p.wait("tensor", sVE, m_rms)

                            def t_ot(te, n_=n_):
                                r = None
                                for c_ in range(2):
                                    r = te.matmul(
                                        out=ptrB[:, 128 * c_:128 * (c_ + 1)],
                                        lhsT=oloc[:, n_, 128 * c_:128 * (c_ + 1)],
                                        rhs=ident, start=True, stop=True)
                                return r
                            m_t2 = p.inc("tensor", t_ot, sTE)
                            p.wait("vector", sTE, m_t2)

                            def v_ot(ve, h=h, n_=n_, cs=cs):
                                r = None
                                for c_ in range(2):
                                    r = ve.tensor_copy(
                                        out=OTB[:, 2 * h + c_, cs],
                                        in_=ptrB[:, 128 * c_:128 * (c_ + 1)])
                                return r
                            corr_ms[n_] = p.inc("vector", v_ot, sVE)
                        m_otb = corr_ms[NCH_ - 1]

                    p.wait("tensor", sVE, m_otb)
                    p.wait("tensor", sQ, m_wob)
                    wadds = [0] * NT
                    for tt in range(NT):
                        pb_ = pPr if tt % 2 == 0 else pX
                        if tt >= 2:
                            p.wait("tensor", sVE, wadds[tt - 2])

                        def t_wo3(te, tt=tt, pb_=pb_):
                            r = None
                            for oh in range(2):
                                for oc in range(8):
                                    r = te.matmul(
                                        out=pb_[:, oh],
                                        lhsT=OTB[:, oc, 128 * tt:128 * (tt + 1)],
                                        rhs=WoB[:, oc, 512 * oh:512 * (oh + 1)],
                                        start=(oc == 0), stop=(oc == 7))
                            return r
                        m_t = p.inc("tensor", t_wo3, sTE)
                        p.wait("vector", sTE, m_t)

                        def v_wo3(ve, tt=tt, pb_=pb_):
                            ve.tensor_add(out=x_res[:, tt, 0:512],
                                          in0=x_res[:, tt, 0:512], in1=pb_[:, 0])
                            return ve.tensor_add(out=x_res[:, tt, 512:1024],
                                                 in0=x_res[:, tt, 512:1024],
                                                 in1=pb_[:, 1])
                        wadds[tt] = p.inc("vector", v_wo3, sVE)
                    phase_barrier()
                    p.flush()

            if tap_phase == "p3":
                emit_store(p, x_res, tap, sO, sVE)
                p.flush()
                return nc

        p.serial_te = None
        if upto >= 4:
            emit_ffn("F2")

        emit_store(p, x_res, out, sO, sVE)
        p.flush()
    return nc


def emit_store(p, x_res, dram, sO, sVE):
    p.wait("sync", sVE, sVE.n)
    m = p.inc("sync", lambda sy: sy.dma_start(
        out=dram.rearrange("(n p) f -> p n f", p=128), in_=x_res), sO, 16)
    p.wait("sync", sO, m)


def _prep_inputs(inp):
    f32 = np.float32
    x = np.ascontiguousarray(inp["L0_attn_a_x"], dtype=f32)
    aWq = np.ascontiguousarray(inp["L0_attn_a_Wq"], f32)
    aWk = np.ascontiguousarray(inp["L0_attn_a_Wk"], f32)
    aWv = np.ascontiguousarray(inp["L0_attn_a_Wv"], f32)
    aWo = np.ascontiguousarray(inp["L0_attn_a_Wo"], f32)
    bWq = np.ascontiguousarray(inp["L0_attn_b_Wq"], f32)
    bWk = np.ascontiguousarray(inp["L0_attn_b_Wk"], f32)
    bWv = np.ascontiguousarray(inp["L0_attn_b_Wv"], f32)
    bWo = np.ascontiguousarray(inp["L0_attn_b_Wo"], f32)

    def conv_arr(w):
        return np.ascontiguousarray(
            w.reshape(8, 128, 4).transpose(1, 0, 2), f32)
    convw = np.stack([conv_arr(inp["L0_attn_b_conv_wq"]),
                      conv_arr(inp["L0_attn_b_conv_wk"]),
                      conv_arr(inp["L0_attn_b_conv_wv"])], axis=1)
    onorm_bc = np.ascontiguousarray(
        np.tile(inp["L0_attn_b_onorm_w"].astype(f32)[None, :], (128, 1)))
    gwT = np.ascontiguousarray(inp["L0_ffn_gate_weight"].astype(f32).T)
    esc_bc = np.ascontiguousarray(
        np.tile(inp["L0_ffn_e_score_correction_bias"].astype(f32)[None, :], (128, 1)))
    guT = np.ascontiguousarray(
        inp["L0_ffn_experts_gate_up"].astype(f32).transpose(0, 2, 1))
    dnT = np.ascontiguousarray(
        inp["L0_ffn_experts_down"].astype(f32).transpose(0, 2, 1))
    sgT = np.ascontiguousarray(inp["L0_ffn_shared_gate_w"].astype(f32).T)
    suT = np.ascontiguousarray(inp["L0_ffn_shared_up_w"].astype(f32).T)
    sdT = np.ascontiguousarray(inp["L0_ffn_shared_down_w"].astype(f32).T)

    identc = np.eye(128, dtype=ml_dtypes.bfloat16)
    stril_c = np.tril(np.ones((128, 128), f32), -1)
    striu_c = np.triu(np.ones((128, 128), f32), 1)
    triu_i_c = np.triu(np.ones((128, 128), f32), 0)

    in_maps = []
    for c in range(N_CORES):
        b_, r_ = c // 4, c % 4
        hsel_c = np.zeros((128, 8), f32)
        if r_ > 0:
            hsel_c[:, c - 1] = 1.0
        ssel_c = np.zeros((128, 6), f32)
        if r_ > 0:
            ssel_c[:, 3 * b_ + (r_ - 1)] = 1.0
        in_maps.append(dict(
            xb=np.ascontiguousarray(x[b_]),
            x_own=np.ascontiguousarray(x[b_, 2048 * r_:2048 * (r_ + 1)]),
            aWq=aWq, aWk=aWk, aWv=aWv, aWo=aWo,
            bWq=bWq, bWk=bWk, bWv=bWv, bWo=bWo,
            convw=convw, onorm_bc=onorm_bc,
            gwTd=gwT, esc_bc=esc_bc, guTd=guT, dnTd=dnT,
            sgTd=sgT, suTd=suT, sdTd=sdT,
            identc=identc, stril_c=stril_c, striu_c=striu_c, triu_i_c=triu_i_c,
            hsel_c=hsel_c, ssel_c=ssel_c,
        ))
    return in_maps


def kernel(**inp):
    in_maps = _prep_inputs(inp)
    nc = build_nc(upto=4)
    res = run_bass_kernel_spmd(nc, in_maps, core_ids=list(range(N_CORES)))
    shards = [res.results[c]["out"] for c in range(N_CORES)]
    outp = np.empty((B, T, D), np.float32)
    for c in range(N_CORES):
        b_, r_ = c // 4, c % 4
        outp[b_, 2048 * r_:2048 * (r_ + 1)] = shards[c]
    return outp
